# version 15 (final)
# speedup vs baseline: 1.1267x; 1.0029x over previous
import contextlib as _ctx

import numpy as np
import concourse.bass as bass
import concourse.mybir as mybir
from concourse.bass_utils import run_bass_kernel_spmd

F32 = mybir.dt.float32
BF16 = mybir.dt.bfloat16
AF = mybir.ActivationFunctionType
OP = mybir.AluOpType
AX = mybir.AxisListType

NCORE = 8
D = 2048
FF = 8192
TCORE = 2048
TT = 512
NTT = TCORE // TT
NKC = D // 128
NFC = FF // 128
NH = 16
SEQ = 16384
EPS_LN = 1e-5
EPS_ATT = 1e-6
EMIN, EMAX = -14, 0

import os
N_PHASE = int(os.environ.get("MK_NPHASE", "3"))
QSUBS = int(os.environ.get("MK_QSUBS", "16"))

ENGINES = ("sync", "tensor", "vector", "scalar", "gpsimd")


class Sem:
    def __init__(self, stack, nc, name):
        self.h = stack.enter_context(nc.semaphore(name))
        self.name = name
        self.n = 0


class Plan:
    def __init__(self):
        self.q = {e: [] for e in ENGINES}
        self.last_wait = {e: {} for e in ENGINES}
        self.chain = {}

    def emit(self, engq, waits, f, inc=None, inc_by=1):
        waits = tuple(w for w in waits if w is not None)
        ch = self.chain.get(engq)
        if ch is not None:
            if inc is None:
                waits = waits + ((ch, ch.n),)
                inc = ch
                inc_by = 1
            elif inc is ch:
                waits = waits + ((ch, ch.n),)
        ws = []
        lw = self.last_wait[engq]
        for s, v in waits:
            if v:
                assert v <= s.n, f"wait on future target {s.name}:{v} > {s.n}"
        for s, v in waits:
            if not v:
                continue
            if lw.get(s.name, 0) >= v:
                continue
            lw[s.name] = v
            ws.append((s.h, v))
        tgt = None
        if inc is not None:
            inc.n += inc_by
            tgt = inc.n

        def fn(eng, _ws=tuple(ws), _f=f, _inc=inc, _by=inc_by):
            for h, v in _ws:
                eng.wait_ge(h, v)
            ins = _f(eng)
            if _inc is not None:
                ins.then_inc(_inc.h, _by)
        self.q[engq].append(fn)
        return tgt


class Ring:

    def __init__(self, sems):
        self.sems = sems
        self.free = [()] * len(sems)
        self.i = 0

    def __len__(self):
        return len(self.sems)

    def acquire(self):
        j = self.i % len(self.sems)
        self.i += 1
        return j, self.sems[j], self.free[j]

    def release(self, j, sem, val):
        self.free[j] = ((sem, val),)


class PsumPool:
    def __init__(self, aps):
        self.aps = aps
        self.free = [()] * len(aps)
        self.i = 0

    def acquire(self):
        j = self.i % len(self.aps)
        self.i += 1
        return j, self.aps[j], self.free[j]

    def release(self, j, sem, val):
        self.free[j] = ((sem, val),)


def build_program():
    stack = _ctx.ExitStack()
    nc = bass.Bass("TRN2", target_bir_lowering=False)
    P = Plan()

    def din(name, shape, dt=F32):
        return nc.dram_tensor(name, list(shape), dt, kind="ExternalInput").ap()

    x_in = din("x", (D, TCORE))
    wqt = din("wqt", (D, D)); wkt = din("wkt", (D, D))
    wvt = din("wvt", (D, D)); wot = din("wot", (D, D))
    upt = din("upt", (FF // NCORE, D))
    dnt = din("dnt", (D // NCORE, FF))
    bq_col = din("bq_col", (128, NKC)); bo_col = din("bo_col", (128, NKC))
    bk_row = din("bk_row", (1, D)); bv_row = din("bv_row", (1, D))
    bup_col = din("bup_col", (128, NFC)); bdn_col = din("bdn_col", (128, NKC))
    pos_row = din("pos_row", (1, TCORE)); pos_col = din("pos_col", (128, NTT * 4))
    ladder_in = din("ladder", (128, EMAX - EMIN + 1))
    sel_in = din("sel", (16, NH * 128))
    eye_in = din("eye", (128, 128), BF16)

    y_out = nc.dram_tensor("y", [D, TCORE], F32, kind="ExternalOutput").ap()

    def scr(name, shape, dt):
        return nc.dram_tensor(name, list(shape), dt).ap()

    wup_loc = scr("wup_loc", (4, NKC, 128, 256), BF16)
    wup_ag = [nc.dram_tensor("wup_ag%d" % p, [NCORE, NKC, 128, 256], BF16,
                             addr_space="Shared").ap() for p in range(4)]
    wdn_loc = scr("wdn_loc", (4, 16, 128, 256), BF16)
    wdn_ag = [nc.dram_tensor("wdn_ag%d" % p, [NCORE, 16, 128, 256], BF16,
                             addr_space="Shared").ap() for p in range(4)]
    wq_b = scr("wq_b", (8, NKC, 128, 256), BF16)
    wo_b = scr("wo_b", (8, NKC, 128, 256), BF16)
    wk_b = scr("wk_b", (D, D), BF16)
    wv_b = scr("wv_b", (D, D), BF16)
    x1_scr = scr("x1s", (D, TCORE), F32)
    x2_scr = scr("x2s", (D, TCORE), F32)
    q_scr = scr("q_scr", (NH, NTT, 128, TT), BF16)
    kv_in = scr("kv_in", (128, NH * 129), F32)
    kv_out = nc.dram_tensor("kv_out", [128, NH * 129], F32, addr_space="Shared").ap()

    all_sems = []

    def mksem(name):
        s = Sem(stack, nc, name)
        all_sems.append(s)
        return s

    s_const = mksem("s_const")
    s_cast = mksem("s_cast")
    s_xbf = mksem("s_xbf")
    s_kv = mksem("s_kv")
    s_kvld = mksem("s_kvld")
    s_mm = mksem("s_mm")
    s_ev = mksem("s_ev")
    s_es = mksem("s_es")
    s_agu = mksem("s_agu")
    s_agd = mksem("s_agd")
    s_ar = mksem("s_ar")
    sd = [mksem("sd%d" % i) for i in range(16)]
    ss = [mksem("ss%d" % i) for i in range(8)]

    P.chain["vector"] = s_ev
    P.chain["scalar"] = s_es

    def barrier():
        tots = tuple((s, s.n) for s in all_sems)
        for e in ENGINES:
            P.emit(e, tots, lambda eng: eng.nop())

    sb = lambda name, shape, dt: stack.enter_context(nc.sbuf_tensor(name, list(shape), dt))
    ones_col_bf = sb("ones_col_bf", (128, 1), BF16)
    ones_row_f = sb("ones_row_f", (1, 128), F32)
    eye_bf = sb("eye_bf", (128, 128), BF16)
    bqc = sb("bqc", (128, NKC), F32); boc = sb("boc", (128, NKC), F32)
    bupc = sb("bupc", (128, NFC), F32); bdnc = sb("bdnc", (128, NKC), F32)
    posr = sb("posr", (1, TCORE), F32); posc = sb("posc", (128, NTT * 4), F32)
    ladder_sb = sb("ladder_sb", (128, EMAX - EMIN + 1), F32)
    qb_s = sb("qb_s", (128, 1), F32)
    qb_r = sb("qb_r", (128, 1), F32)
    decay_row = posr
    decay_p = sb("decay_p", (128, NTT * 4), F32)

    def cld(out, in_):
        return P.emit("sync", (), lambda eng, o=out, i=in_: eng.dma_start(out=o, in_=i),
                      inc=s_const, inc_by=16)

    cld(eye_bf[:, :], eye_in[:, :])
    cld(bqc[:, :], bq_col[:, :]); cld(boc[:, :], bo_col[:, :])
    cld(bupc[:, :], bup_col[:, :]); cld(bdnc[:, :], bdn_col[:, :])
    cld(posr[:, :], pos_row[:, :]); cld(posc[:, :], pos_col[:, :])
    cld(ladder_sb[:, :], ladder_in[:, :])
    t_const = s_const.n

    P.emit("vector", (), lambda eng: eng.memset(ones_col_bf[:, :], 1.0))
    P.emit("vector", (), lambda eng: eng.memset(ones_row_f[:, :], 1.0))
    import math as _math
    P.emit("vector", (), lambda eng: eng.memset(qb_s[:, :], (EMIN - 3) * _math.log(2.0)))
    P.emit("vector", (), lambda eng: eng.memset(qb_r[:, :], (3 - EMIN) * _math.log(2.0)))
    t_ones = s_ev.n

    P.emit("scalar", ((s_const, t_const),), lambda eng: eng.activation(
        out=decay_row[:, :], in_=posr[:, :], func=AF.Exp, scale=-(2.0 ** -14)))
    t_decrow = s_es.n
    P.emit("scalar", (), lambda eng: eng.activation(
        out=decay_p[:, :], in_=posc[:, :], func=AF.Exp, scale=-(2.0 ** -14)))
    t_decp = s_es.n

    wqt_v = wqt.rearrange("(m p) c -> m p c", p=128)
    wot_v = wot.rearrange("(m p) c -> m p c", p=128)
    cast_jobs = []
    for r in range(16):
        cast_jobs.append((wk_b[r * 128:(r + 1) * 128, :], wkt[r * 128:(r + 1) * 128, :]))
        cast_jobs.append((wv_b[r * 128:(r + 1) * 128, :], wvt[r * 128:(r + 1) * 128, :]))
    for pr in range(8):
        cast_jobs.append((wq_b[pr], wqt_v[:, :, pr * 256:(pr + 1) * 256]))
        cast_jobs.append((wo_b[pr], wot_v[:, :, pr * 256:(pr + 1) * 256]))

    def emit_cast_jobs(n):
        while cast_jobs and n > 0:
            o, i = cast_jobs.pop(0)
            P.emit("gpsimd", (), lambda eng, o=o, i=i: eng.dma_start(out=o, in_=i),
                   inc=s_cast, inc_by=16)
            n -= 1
    t_cast = 48 * 16

    q_es = _ctx.ExitStack()
    QSA = lambda n, sh, dt: q_es.enter_context(nc.sbuf_tensor(n, list(sh), dt))
    HS = 1024
    qws = [QSA("qw%d" % i, (128, HS), F32) for i in range(2)]
    NLAD = EMAX - EMIN + 1
    q_amax = QSA("q_amax", (128, HS // 32), F32)
    q_m3 = QSA("q_m3", (128, (HS // 32) * NLAD), BF16)
    q_cnt = QSA("q_cnt", (128, HS // 32), F32)
    q_s = QSA("q_s", (128, HS // 32), F32)
    q_r = QSA("q_r", (128, HS // 32), F32)
    q_y2 = QSA("q_y2", (128, HS), F32)
    q_t1 = QSA("q_t1", (128, HS), BF16)
    q_t2 = QSA("q_t2", (128, HS), BF16)
    q_sg = QSA("q_sg", (128, HS), BF16)
    q_wos = [QSA("q_wo%d" % i, (128, HS), BF16) for i in range(2)]
    stgs = [QSA("q_st%d" % i, (128, HS), BF16) for i in range(2)]
    pstp = [q_es.enter_context(nc.psum_tensor("ps_tp%d" % i, [128, 128], BF16))
            for i in range(2)]
    q_state = {
        "qw_ring": Ring(sd[14:16]),
        "stg_ring": Ring(ss[6:8]),
        "tp_free": [(), ()],
        "prev": {"asg": 0, "chain": 0, "tp": {}, "k": 0},
    }

    def emit_quant_half(sub):
        qw_ring = q_state["qw_ring"]; stg_ring = q_state["stg_ring"]
        tp_free = q_state["tp_free"]; prev = q_state["prev"]
        j, lsem, w = qw_ring.acquire()
        if sub[0] == "up":
            r, h = sub[1], sub[2]
            srcap = upt[r * 128:(r + 1) * 128, h * HS:(h + 1) * HS]
        else:
            rd, qf, qh = sub[1], sub[2], sub[3]
            srcap = dnt[rd * 128:(rd + 1) * 128, qf * 2048 + qh * HS:qf * 2048 + (qh + 1) * HS]
        tl = P.emit("scalar", w, lambda eng, o=qws[j], s=srcap: eng.dma_start(
            out=o[:, :], in_=s), inc=lsem, inc_by=16)
        qw = qws[j]
        qw3 = qw[:, :].rearrange("p (b k) -> p b k", k=32)
        P.emit("vector", ((lsem, tl),), lambda eng, i=qw3: eng.tensor_reduce(
            out=q_amax[:, :], in_=i, axis=AX.X, op=OP.max, apply_absolute_value=True))
        am3 = q_amax[:, :].rearrange("p b -> p b ()").broadcast_to((128, HS // 32, NLAD))
        lad3 = ladder_sb[:, :].rearrange("p l -> p () l").broadcast_to((128, HS // 32, NLAD))
        m3v = q_m3[:, :].rearrange("p (b l) -> p b l", l=NLAD)
        P.emit("vector", ((s_const, t_const),), lambda eng, a=am3, b=lad3, o=m3v:
               eng.tensor_tensor(out=o, in0=a, in1=b, op=OP.is_ge))
        P.emit("vector", ((s_es, prev.get("tsr", 0)),), lambda eng, i=m3v: eng.tensor_reduce(
            out=q_cnt[:, :], in_=i, axis=AX.X, op=OP.add))
        t_cnt = s_ev.n
        LN2C = float(np.log(2.0))
        P.emit("scalar", ((s_ev, t_cnt), (s_ev, prev["chain"]), (s_ev, t_ones)),
               lambda eng: eng.activation(
                   out=q_s[:, :], in_=q_cnt[:, :], func=AF.Exp, scale=LN2C,
                   bias=qb_s[:, 0:1]))
        P.emit("scalar", (), lambda eng: eng.activation(
            out=q_r[:, :], in_=q_cnt[:, :], func=AF.Exp, scale=-LN2C,
            bias=qb_r[:, 0:1]))
        t_sr = s_es.n
        prev["tsr"] = t_sr
        rb = q_r[:, :].broadcast_to((128, HS // 32, 32))
        P.emit("vector", ((s_es, prev["asg"]), (s_es, t_sr)), lambda eng, a=qw3, b=rb:
               eng.tensor_tensor(out=a, in0=a, in1=b, op=OP.mult))
        t_y = s_ev.n
        P.emit("scalar", ((s_ev, t_y), (s_ev, prev["chain"])), lambda eng, i=qw: eng.activation(
            out=q_sg[:, :], in_=i[:, :], func=AF.Sign))
        prev["asg"] = s_es.n
        P.emit("vector", (), lambda eng, i=qw: eng.tensor_tensor(
            out=q_y2[:, :], in0=i[:, :], in1=i[:, :], op=OP.mult))
        P.emit("vector", (), lambda eng: eng.tensor_scalar(
            out=q_t1[:, :], in0=q_y2[:, :], scalar1=0.0625, scalar2=None, op0=OP.is_ge))
        for th2, wgt in ((0.5625, 1.0), (1.5625, 1.0), (3.0625, 1.0),
                         (6.25, 2.0), (12.25, 2.0), (25.0, 4.0)):
            if wgt == 1.0:
                P.emit("vector", (), lambda eng, th2=th2: eng.tensor_scalar(
                    out=q_t2[:, :], in0=q_y2[:, :], scalar1=th2, scalar2=None,
                    op0=OP.is_ge))
            else:
                P.emit("vector", (), lambda eng, th2=th2, wgt=wgt: eng.tensor_scalar(
                    out=q_t2[:, :], in0=q_y2[:, :], scalar1=th2, scalar2=wgt,
                    op0=OP.is_ge, op1=OP.mult))
            P.emit("vector", (), lambda eng: eng.tensor_tensor(
                out=q_t1[:, :], in0=q_t1[:, :], in1=q_t2[:, :], op=OP.add))
        P.emit("vector", ((s_es, prev["asg"]),), lambda eng: eng.scalar_tensor_tensor(
            out=q_t2[:, :], in0=q_t1[:, :], scalar=0.5, in1=q_sg[:, :],
            op0=OP.mult, op1=OP.mult))
        ksub = prev["k"]
        prev["k"] = ksub + 1
        q_wo = q_wos[ksub % 2]
        w13 = q_t2[:, :].rearrange("p (b k) -> p b k", k=32)
        sb3 = q_s[:, :].broadcast_to((128, HS // 32, 32))
        wo3 = q_wo[:, :].rearrange("p (b k) -> p b k", k=32)
        P.emit("vector", (prev["tp"].get(ksub % 2),), lambda eng, a=w13, b=sb3, o=wo3:
               eng.tensor_tensor(out=o, in0=a, in1=b, op=OP.mult))
        t_wqd = s_ev.n
        prev["chain"] = t_wqd
        qw_ring.release(j, s_ev, t_wqd)

        jstg, stsem, wst = stg_ring.acquire()
        stg = stgs[jstg]
        t_tp_es = 0
        for p in range(8):
            jt = p % 2
            P.emit("tensor", ((s_ev, t_wqd), (s_const, t_const)) + tuple(tp_free[jt]),
                   lambda eng, p=p, ps=pstp[jt]: eng.transpose(
                       ps[:, :], q_wo[:, p * 128:(p + 1) * 128], eye_bf[:, :]), inc=s_mm)
            tmv = s_mm.n
            P.emit("scalar", ((s_mm, tmv),) + (tuple(wst) if p == 0 else ()),
                   lambda eng, p=p, ps=pstp[jt], stg=stg: eng.activation(
                       out=stg[:, p * 128:(p + 1) * 128], in_=ps[:, :], func=AF.Copy))
            tp_free[jt] = ((s_es, s_es.n),)
            t_tp_es = s_es.n
        prev["tp"][ksub % 2] = (s_mm, s_mm.n)

        stg_v = stg[:, :].rearrange("p (j c) -> p j c", j=8)
        if sub[0] == "up":
            dst = wup_loc[r // 2].rearrange("k dd c -> dd k c")[
                :, h * 8:(h + 1) * 8, (r % 2) * 128:(r % 2) * 128 + 128]
        else:
            dst = wdn_loc[qf].rearrange("j ff c -> ff j c")[
                :, qh * 8:(qh + 1) * 8, rd * 128:rd * 128 + 128]
        tst = P.emit("scalar", ((s_es, t_tp_es),),
                     lambda eng, dd=dst, s=stg_v: eng.dma_start(out=dd, in_=s),
                     inc=stsem, inc_by=16)
        stg_ring.release(jstg, stsem, tst)

    def emit_quant_ag(kind, idx):
        wag = tuple((sm, sm.n) for sm in ss[6:8])
        if kind == "up":
            P.emit("gpsimd", wag, lambda eng, pr=idx: eng.collective_compute(
                "AllGather", OP.bypass, replica_groups=[list(range(NCORE))],
                ins=[wup_loc[pr]], outs=[wup_ag[pr]]), inc=s_agu)
        else:
            P.emit("gpsimd", wag, lambda eng, qf=idx: eng.collective_compute(
                "AllGather", OP.bypass, replica_groups=[list(range(NCORE))],
                ins=[wdn_loc[qf]], outs=[wdn_ag[qf]]), inc=s_agd)

    def pump_quant_up(g):
        while q_state.setdefault("up_done", 0) <= g:
            gg = q_state["up_done"]
            for r in (2 * gg, 2 * gg + 1):
                for h in range(2):
                    emit_quant_half(("up", r, h))
            emit_quant_ag("up", gg)
            q_state["up_done"] = gg + 1

    def pump_quant_down():
        if q_state.get("dn_done"):
            return
        pump_quant_up(3)
        for qf in range(4):
            for rd in range(2):
                for qh in range(2):
                    emit_quant_half(("dn", rd, qf, qh))
            emit_quant_ag("dn", qf)
        q_state["dn_done"] = True

    barrier()

    store_tgt = {}
    F_ORDER = [c * 8 + 2 * g + s for g in range(4) for c in range(8) for s in range(2)]

    def emit_ln(src, src_name, tt, pool, xln, xring, xrbufs, xcs, sqs, stm, str_,
                tv0, tv1, guards):
        jm, psm, wm = pool.acquire()
        js, pss, wsq = pool.acquire()
        sq_guard = guards.get("sq_guard", {})
        ssw = guards.get("src_store", {})
        t_mmm = t_msq = 0
        for m in range(NKC):
            jx, xsem, wx = xring.acquire()
            tl = P.emit("sync", tuple(wx) + (ssw.get(m),),
                        lambda eng, m=m, jx=jx, tt=tt: eng.dma_start(
                            out=xrbufs[jx][:, :],
                            in_=src[m * 128:(m + 1) * 128, tt * TT:(tt + 1) * TT]),
                        inc=xsem, inc_by=16)
            P.emit("scalar", ((xsem, tl), sq_guard.get(("xc", m % 2))),
                   lambda eng, m=m, jx=jx: eng.activation(
                       out=xcs[m % 2][:, :], in_=xrbufs[jx][:, :], func=AF.Copy))
            P.emit("scalar", (sq_guard.get(("sq", m % 2)),),
                   lambda eng, m=m, jx=jx: eng.activation(
                       out=sqs[m % 2][:, :], in_=xrbufs[jx][:, :], func=AF.Square))
            t_sqm = s_es.n
            xring.release(jx, s_es, t_sqm)
            P.emit("tensor", ((s_es, t_sqm),) + (tuple(wm) if m == 0 else ()),
                   lambda eng, m=m: eng.matmul(
                       psm[0:1, :], ones_col_bf[:, :], xcs[m % 2][:, :],
                       start=(m == 0), stop=(m == NKC - 1)), inc=s_mm)
            t_mmm = s_mm.n
            sq_guard[("xc", m % 2)] = (s_mm, t_mmm)
            P.emit("tensor", (tuple(wsq) if m == 0 else ()),
                   lambda eng, m=m: eng.matmul(
                       pss[0:1, :], ones_col_bf[:, :], sqs[m % 2][:, :],
                       start=(m == 0), stop=(m == NKC - 1)), inc=s_mm)
            t_msq = s_mm.n
            sq_guard[("sq", m % 2)] = (s_mm, t_msq)
        P.emit("vector", ((s_mm, t_mmm),), lambda eng: eng.tensor_scalar(
            out=stm[:, :], in0=psm[0:1, :], scalar1=1.0 / D, scalar2=None, op0=OP.mult))
        P.emit("vector", ((s_mm, t_msq),), lambda eng: eng.tensor_scalar(
            out=str_[:, :], in0=pss[0:1, :], scalar1=1.0 / D, scalar2=None, op0=OP.mult))
        P.emit("vector", (), lambda eng: eng.tensor_tensor(
            out=tv0[0:1, :], in0=stm[:, :], in1=stm[:, :], op=OP.mult))
        P.emit("vector", (), lambda eng: eng.tensor_tensor(
            out=str_[:, :], in0=str_[:, :], in1=tv0[0:1, :], op=OP.subtract))
        P.emit("vector", (), lambda eng: eng.tensor_scalar(
            out=str_[:, :], in0=str_[:, :], scalar1=EPS_LN, scalar2=None, op0=OP.add))
        P.emit("vector", (), lambda eng: eng.reciprocal(out=tv0[0:1, :], in_=str_[:, :]))
        t_rv = s_ev.n
        P.emit("scalar", ((s_ev, t_rv),), lambda eng: eng.activation(
            out=tv1[0:1, :], in_=tv0[0:1, :], func=AF.Ln))
        P.emit("scalar", (), lambda eng: eng.activation(
            out=tv0[0:1, :], in_=tv1[0:1, :], func=AF.Exp, scale=0.5))
        t_seed = s_es.n
        P.emit("vector", ((s_es, t_seed),), lambda eng: eng.tensor_tensor(
            out=tv1[0:1, :], in0=tv0[0:1, :], in1=tv0[0:1, :], op=OP.mult))
        P.emit("vector", (), lambda eng: eng.tensor_tensor(
            out=tv1[0:1, :], in0=tv1[0:1, :], in1=str_[:, :], op=OP.mult))
        P.emit("vector", (), lambda eng: eng.tensor_scalar(
            out=tv1[0:1, :], in0=tv1[0:1, :], scalar1=-0.5, scalar2=1.5, op0=OP.mult, op1=OP.add))
        P.emit("vector", (), lambda eng: eng.tensor_tensor(
            out=str_[:, :], in0=tv0[0:1, :], in1=tv1[0:1, :], op=OP.mult))
        t_rstd = s_ev.n
        pool.release(jm, s_ev, t_rstd)
        pool.release(js, s_ev, t_rstd)
        jr1, psr1, wr1 = pool.acquire()
        P.emit("tensor", ((s_ev, t_rstd),) + tuple(wr1), lambda eng: eng.matmul(
            psr1[:, :], ones_row_f[:, :], stm[:, :], start=True, stop=True), inc=s_mm)
        jr2, psr2, wr2 = pool.acquire()
        P.emit("tensor", tuple(wr2), lambda eng: eng.matmul(
            psr2[:, :], ones_row_f[:, :], str_[:, :], start=True, stop=True), inc=s_mm)
        t_rep = s_mm.n
        t_xln = []
        for m in range(NKC):
            jx, xsem, wx = xring.acquire()
            tl = P.emit("sync", tuple(wx) + (ssw.get(m),),
                        lambda eng, m=m, jx=jx, tt=tt: eng.dma_start(
                            out=xrbufs[jx][:, :],
                            in_=src[m * 128:(m + 1) * 128, tt * TT:(tt + 1) * TT]),
                        inc=xsem, inc_by=16)
            P.emit("vector", ((s_mm, t_rep), (xsem, tl)), lambda eng, jx=jx: eng.tensor_tensor(
                out=xrbufs[jx][:, :], in0=xrbufs[jx][:, :], in1=psr1[:, :], op=OP.subtract))
            P.emit("vector", (), lambda eng, m=m, jx=jx: eng.tensor_tensor(
                out=xln[:, m * TT:(m + 1) * TT], in0=xrbufs[jx][:, :], in1=psr2[:, :],
                op=OP.mult))
            t_xln.append(s_ev.n)
            xring.release(jx, s_ev, s_ev.n)
        pool.release(jr1, s_ev, s_ev.n)
        pool.release(jr2, s_ev, s_ev.n)
        return t_xln

    def emit_ffn(src, src_name, dst, dst_name, phase):
        first_ffn = phase == "ffn1"
        es = _ctx.ExitStack()
        SA = lambda n, sh, dt: es.enter_context(nc.sbuf_tensor(phase + n, list(sh), dt))
        PA = lambda n: es.enter_context(nc.psum_tensor(phase + n, [128, TT], F32))
        xln = SA("_xln", (128, NKC * TT), BF16)
        hbuf = SA("_h", (128, NFC * TT), BF16)
        xrbufs = [SA("_xr%d" % i, (128, TT), F32) for i in range(3)]
        xcs = [SA("_xc%d" % i, (128, TT), BF16) for i in range(2)]
        sqs = [SA("_sq%d" % i, (128, TT), BF16) for i in range(2)]
        st0 = SA("_st0", (1, TT), F32); st1 = SA("_st1", (1, TT), F32)
        tv0 = SA("_tv0", (1, TT), F32); tv1 = SA("_tv1", (1, TT), F32)
        tmps = [SA("_tm%d" % i, (128, TT), F32) for i in range(2)]
        obufs = [SA("_ob%d" % i, (128, TT), F32) for i in range(2)]
        wu = [SA("_wu%d" % i, (128, NKC * 256), BF16) for i in range(3)]
        wd = [SA("_wd%d" % i, (128, 16 * 256), BF16) for i in range(5)]
        pool = PsumPool([PA("_p%d" % i) for i in range(6)])
        if True:
            xring = Ring(sd[0:3])
            wu_ring = Ring(sd[4:6] + [sd[12]])
            wd_ring = Ring(sd[6:10] + [sd[13]])
            ob_ring = Ring(ss[0:2])
            sq_guard = {}
            prev_gelu = {}

            def do_ln(tt):
                guards = {"sq_guard": sq_guard}
                if src_name != "x":
                    guards["src_store"] = {m: store_tgt[(src_name, tt, m)] for m in range(NKC)}
                return emit_ln(src, src_name, tt, pool, xln, xring, xrbufs, xcs,
                               sqs, st0, st1, tv0, tv1, guards)

            if first_ffn:
                pump_quant_up(0)
            t_xln = do_ln(0)
            for tt in range(NTT):
                for fi, f in enumerate(F_ORDER):
                    g, s = divmod(f % 8, 2)
                    c = f // 8
                    if first_ffn and tt == 0 and fi % 16 == 0:
                        pump_quant_up(min(g + 1, 3))
                    if s == 0:
                        jw, wsem, ww = wu_ring.acquire()
                        agw = ((s_agu, g + 1),) if first_ffn else ()
                        t_wu = P.emit("sync", tuple(ww) + agw,
                                      lambda eng, jw=jw, c=c, g=g: eng.dma_start(
                                          out=wu[jw][:, :].rearrange("p (k c) -> p k c", c=256),
                                          in_=wup_ag[g][c].rearrange("k p c -> p k c")),
                                      inc=wsem, inc_by=16)
                        cur_wu = (jw, wsem, t_wu)
                    jw, wsem, t_wu = cur_wu
                    jp, psu, wp = pool.acquire()
                    for k in range(NKC):
                        w = ((wsem, t_wu), (s_ev, t_xln[k]))
                        if k == 0:
                            w = w + tuple(wp)
                        P.emit("tensor", w, lambda eng, k=k, jw=jw, s=s, ps=psu: eng.matmul(
                            ps[:, :], wu[jw][:, k * 256 + s * 128:k * 256 + s * 128 + 128],
                            xln[:, k * TT:(k + 1) * TT],
                            start=(k == 0), stop=(k == NKC - 1)), inc=s_mm)
                    t_up = s_mm.n
                    if s == 1:
                        wu_ring.release(jw, s_mm, t_up)
                    P.emit("scalar", ((s_mm, t_up), (s_const, t_const), prev_gelu.get(f)),
                           lambda eng, f=f, ps=psu: eng.activation(
                               out=hbuf[:, f * TT:(f + 1) * TT], in_=ps[:, :],
                               func=AF.Gelu_apprx_tanh, bias=bupc[:, f:f + 1]))
                    pool.release(jp, s_es, s_es.n)
                t_h_all = s_es.n
                if first_ffn and tt == 0:
                    pump_quant_down()

                t_xln_next = None
                for dop in range(8):
                    if first_ffn and tt <= 2:
                        emit_cast_jobs(2)
                    t_wd = []
                    wd_slots = []
                    for qf in range(4):
                        jw, wsem, ww = wd_ring.acquire()
                        agw = ((s_agd, qf + 1),) if first_ffn else ()
                        t_ = P.emit("sync", tuple(ww) + agw,
                                    lambda eng, jw=jw, dop=dop, qf=qf: eng.dma_start(
                                        out=wd[jw][:, :].rearrange("p (j c) -> p j c", c=256),
                                        in_=wdn_ag[qf][dop].rearrange("j p c -> p j c")),
                                    inc=wsem, inc_by=16)
                        t_wd.append((wsem, t_))
                        wd_slots.append(jw)
                    for half in range(2):
                        do = dop * 2 + half
                        jp, psd, wp = pool.acquire()
                        for f in range(NFC):
                            qf, jj = divmod(f, 16)
                            w = []
                            if f == 0:
                                w += list(wp) + [(s_es, t_h_all)]
                            if jj == 0:
                                w.append(t_wd[qf])
                                if first_ffn:
                                    w.append((s_agd, qf + 1))
                            last_use = (half == 1 and jj == 15)
                            tmm = P.emit("tensor", tuple(w),
                                   lambda eng, f=f, qf=qf, jj=jj, half=half, ps=psd,
                                          sl=wd_slots: eng.matmul(
                                       ps[:, :],
                                       wd[sl[qf]][:, jj * 256 + half * 128:jj * 256 + half * 128 + 128],
                                       hbuf[:, f * TT:(f + 1) * TT],
                                       start=(f == 0), stop=(f == NFC - 1)),
                                   inc=(s_mm if (last_use or f == NFC - 1) else None))
                            if last_use:
                                wd_ring.release(wd_slots[qf], s_mm, tmm)
                        t_dn = s_mm.n
                        P.emit("vector", ((s_mm, t_dn), (s_const, t_const)),
                               lambda eng, do=do, ps=psd: eng.tensor_scalar(
                                   out=tmps[do % 2][:, :], in0=ps[:, :],
                                   scalar1=bdnc[:, do:do + 1], scalar2=0.5,
                                   op0=OP.add, op1=OP.mult))
                        t_tmp = s_ev.n
                        pool.release(jp, s_ev, t_tmp)
                        jx, xsem, wx = xring.acquire()
                        wsrc = store_tgt[(src_name, tt, do)] if src_name != "x" else None
                        tl = P.emit("sync", tuple(wx) + (wsrc,),
                                    lambda eng, do=do, jx=jx, tt=tt: eng.dma_start(
                                        out=xrbufs[jx][:, :],
                                        in_=src[do * 128:(do + 1) * 128, tt * TT:(tt + 1) * TT]),
                                    inc=xsem, inc_by=16)
                        jo, osem, wo_ = ob_ring.acquire()
                        P.emit("vector", ((xsem, tl),) + tuple(wo_),
                               lambda eng, do=do, jx=jx, jo=jo: eng.tensor_tensor(
                                   out=obufs[jo][:, :], in0=tmps[do % 2][:, :],
                                   in1=xrbufs[jx][:, :], op=OP.add))
                        t_res = s_ev.n
                        xring.release(jx, s_ev, t_res)
                        tstv = P.emit("gpsimd", ((s_ev, t_res),),
                                      lambda eng, do=do, jo=jo, tt=tt: eng.dma_start(
                                          out=dst[do * 128:(do + 1) * 128, tt * TT:(tt + 1) * TT],
                                          in_=obufs[jo][:, :]),
                                      inc=osem, inc_by=16)
                        ob_ring.release(jo, osem, tstv)
                        store_tgt[(dst_name, tt, do)] = (osem, tstv)
                    if dop == 6 and tt + 1 < NTT:
                        t_xln_next = do_ln(tt + 1)
                for f in range(NFC):
                    prev_gelu[f] = (s_mm, s_mm.n)
                if t_xln_next is not None:
                    t_xln = t_xln_next
        es.close()
        if first_ffn:
            q_es.close()

    def emit_attn():
        es0 = _ctx.ExitStack()
        kv_sb = es0.enter_context(nc.sbuf_tensor("kv_sb", [128, NH * 129], F32))
        kv_bf = es0.enter_context(nc.sbuf_tensor("kv_bf", [128, NH * 129], BF16))

        es = _ctx.ExitStack()
        SA = lambda n, sh, dt: es.enter_context(nc.sbuf_tensor(n, list(sh), dt))
        xln = SA("a1_xln", (128, NKC * TT), BF16)
        ktm = SA("a1_k", (128, 4 * D), BF16)
        vtm = SA("a1_v", (128, 4 * D), BF16)
        xrbufs = [SA("a1_xr%d" % i, (128, TT), F32) for i in range(3)]
        xcs = [SA("a1_xc%d" % i, (128, TT), BF16) for i in range(2)]
        sqs = [SA("a1_sq%d" % i, (128, TT), BF16) for i in range(2)]
        st0 = SA("a1_st0", (1, TT), F32); st1 = SA("a1_st1", (1, TT), F32)
        tv0 = SA("a1_tv0", (1, TT), F32); tv1 = SA("a1_tv1", (1, TT), F32)
        tmps = [SA("a1_tm%d" % i, (128, TT), F32) for i in range(2)]
        qts = [SA("a1_qt%d" % i, (128, TT), F32) for i in range(2)]
        qsgs = [SA("a1_qs%d" % i, (128, TT), BF16) for i in range(2)]
        wqs = [SA("a1_wq%d" % i, (128, NKC * 256), BF16) for i in range(2)]
        kvslabs = [SA("a1_kv%d" % i, (128, 8 * 512), BF16) for i in range(5)]
        decay_rep = SA("a1_dr", (128, TCORE), F32)
        bk_rep = SA("a1_bk", (128, D), F32)
        bv_rep = SA("a1_bv", (128, D), F32)
        bkr = SA("a1_bkr", (1, D), F32)
        bvr = SA("a1_bvr", (1, D), F32)
        pkv = es.enter_context(nc.psum_tensor("a1_pkv", [128, 2048], F32))
        pks = es.enter_context(nc.psum_tensor("a1_pks", [128, 16], F32))
        pool = PsumPool([es.enter_context(nc.psum_tensor("a1_p%d" % i, [128, TT], F32))
                         for i in range(3)])
        if True:
            xring = Ring(sd[0:3])
            wq_ring = Ring(sd[3:5])
            kv_ring = Ring(sd[5:9] + [sd[12]])
            qs_ring = Ring(ss[2:4])
            sq_guard = {}
            prev_qd = {}
            prev_kv_tile = 0
            prev_ktmp = {}

            t_bkr = P.emit("sync", (), lambda eng: eng.dma_start(
                out=bkr[:, :], in_=bk_row[:, :]), inc=sd[10], inc_by=16)
            t_bvr = P.emit("sync", ((sd[10], t_bkr),), lambda eng: eng.dma_start(
                out=bvr[:, :], in_=bv_row[:, :]), inc=sd[10], inc_by=16)
            for srcv, dstv, tw in ((decay_row, decay_rep, (s_es, t_decrow)),
                                   (bkr, bk_rep, (sd[10], t_bkr)),
                                   (bvr, bv_rep, (sd[10], t_bvr))):
                for c4 in range(4):
                    jp, psb, wpb = pool.acquire()
                    P.emit("tensor", (tw, (s_ev, t_ones)) + tuple(wpb),
                           lambda eng, s=srcv, c4=c4, ps=psb: eng.matmul(
                               ps[:, :], ones_row_f[:, :], s[:, c4 * 512:(c4 + 1) * 512],
                               start=True, stop=True), inc=s_mm)
                    tmb = s_mm.n
                    P.emit("vector", ((s_mm, tmb),),
                           lambda eng, d=dstv, c4=c4, ps=psb: eng.tensor_copy(
                               out=d[:, c4 * 512:(c4 + 1) * 512], in_=ps[:, :]))
                    pool.release(jp, s_ev, s_ev.n)
            t_reps = s_ev.n

            def do_ln2(tt):
                guards = {
                    "sq_guard": sq_guard,
                    "src_store": {m: store_tgt[("x1", tt, m)] for m in range(NKC)},
                }
                return emit_ln(x1_scr, "x1", tt, pool, xln, xring, xrbufs, xcs,
                               sqs, st0, st1, tv0, tv1, guards)

            t_xln = do_ln2(0)
            for tt in range(NTT):

                for dqp in range(8):
                    jw, wsem, ww = wq_ring.acquire()
                    t_wq_ = P.emit("sync", tuple(ww) + ((s_cast, t_cast),),
                                   lambda eng, jw=jw, dqp=dqp: eng.dma_start(
                                       out=wqs[jw][:, :].rearrange("p (m c) -> p m c", c=256),
                                       in_=wq_b[dqp].rearrange("m p c -> p m c")),
                                   inc=wsem, inc_by=16)
                    for s in range(2):
                        dq = dqp * 2 + s
                        jp, psq, wp = pool.acquire()
                        for m in range(NKC):
                            w = ((wsem, t_wq_), (s_ev, t_xln[m]))
                            if m == 0:
                                w = w + tuple(wp)
                            P.emit("tensor", w, lambda eng, m=m, jw=jw, s=s, ps=psq: eng.matmul(
                                ps[:, :], wqs[jw][:, m * 256 + s * 128:m * 256 + s * 128 + 128],
                                xln[:, m * TT:(m + 1) * TT],
                                start=(m == 0), stop=(m == NKC - 1)), inc=s_mm)
                        t_qmm = s_mm.n
                        if s == 1:
                            wq_ring.release(jw, s_mm, t_qmm)
                        P.emit("scalar", ((s_mm, t_qmm), (s_const, t_const), prev_qd.get(dq % 2)),
                               lambda eng, dq=dq, ps=psq: eng.activation(
                                   out=qts[dq % 2][:, :], in_=ps[:, :], func=AF.Relu,
                                   bias=bqc[:, dq:dq + 1]))
                        t_qrelu = s_es.n
                        pool.release(jp, s_es, t_qrelu)
                        jqs, qsem, wqs_ = qs_ring.acquire()
                        qsg = qsgs[jqs]
                        P.emit("vector", ((s_es, t_qrelu),) + tuple(wqs_),
                               lambda eng, dq=dq, tt=tt, qsg=qsg: eng.tensor_tensor(
                                   out=qsg[:, :], in0=qts[dq % 2][:, :],
                                   in1=decay_rep[:, tt * TT:(tt + 1) * TT], op=OP.mult))
                        t_qd = s_ev.n
                        prev_qd[dq % 2] = (s_ev, t_qd)
                        tq = P.emit("gpsimd", ((s_ev, t_qd),),
                                    lambda eng, dq=dq, tt=tt, qsg=qsg: eng.dma_start(
                                        out=q_scr[dq, tt], in_=qsg[:, :]),
                                    inc=qsem, inc_by=16)
                        store_tgt[("q", tt, dq)] = (qsem, tq)
                        qs_ring.release(jqs, qsem, tq)

                t_xln_next = None
                for d5 in range(4):
                    slabs = {}
                    for mat, wsrc in (("k", wk_b), ("v", wv_b)):
                        for hh in range(2):
                            jw, wsem, ww = kv_ring.acquire()
                            t_ = P.emit("sync", tuple(ww) + ((s_cast, t_cast),),
                                        lambda eng, jw=jw, wsrc=wsrc, hh=hh, d5=d5: eng.dma_start(
                                            out=kvslabs[jw][:, :].rearrange("p (a d) -> p a d", d=512),
                                            in_=wsrc.rearrange("(a p) d -> p a d", p=128)[
                                                :, hh * 8:(hh + 1) * 8,
                                                d5 * 512:(d5 + 1) * 512]),
                                        inc=wsem, inc_by=16)
                            slabs[(mat, hh)] = (jw, wsem, t_)
                    for t1 in range(4):
                        for mat in ("k", "v"):
                            jp, pskv, wp = pool.acquire()
                            for m in range(NKC):
                                hh, mm_ = divmod(m, 8)
                                jw, wsem, t_ = slabs[(mat, hh)]
                                w = ((wsem, t_), (s_ev, t_xln[m]))
                                if m == 0:
                                    w = w + tuple(wp)
                                last_use = (t1 == 3 and mm_ == 7)
                                tmm = P.emit("tensor", w,
                                       lambda eng, m=m, mm_=mm_, jw=jw, t1=t1, ps=pskv: eng.matmul(
                                           ps[:, :],
                                           xln[:, m * TT + t1 * 128:m * TT + t1 * 128 + 128],
                                           kvslabs[jw][:, mm_ * 512:(mm_ + 1) * 512],
                                           start=(m == 0), stop=(m == NKC - 1)),
                                       inc=(s_mm if (last_use or m == NKC - 1) else None))
                                if last_use:
                                    kv_ring.release(jw, s_mm, tmm)
                            t_kvp = s_mm.n
                            if mat == "k":
                                P.emit("vector", ((s_mm, t_kvp), prev_ktmp.get(t1 % 2)),
                                       lambda eng, t1=t1, d5=d5, ps=pskv: eng.tensor_tensor(
                                           out=tmps[t1 % 2][:, :], in0=ps[:, :],
                                           in1=bk_rep[:, d5 * 512:(d5 + 1) * 512], op=OP.add))
                                t_add = s_ev.n
                                pool.release(jp, s_ev, t_add)
                                P.emit("scalar", ((s_ev, t_add), (s_mm, prev_kv_tile)),
                                       lambda eng, t1=t1, d5=d5, tt=tt: eng.activation(
                                           out=ktm[:, t1 * D + d5 * 512:t1 * D + (d5 + 1) * 512],
                                           in_=tmps[t1 % 2][:, :], func=AF.Relu,
                                           scale=decay_p[:, tt * 4 + t1:tt * 4 + t1 + 1]))
                                prev_ktmp[t1 % 2] = (s_es, s_es.n)
                            else:
                                P.emit("vector", ((s_mm, t_kvp), (s_mm, prev_kv_tile)),
                                       lambda eng, t1=t1, d5=d5, ps=pskv: eng.tensor_tensor(
                                           out=vtm[:, t1 * D + d5 * 512:t1 * D + (d5 + 1) * 512],
                                           in0=ps[:, :],
                                           in1=bv_rep[:, d5 * 512:(d5 + 1) * 512], op=OP.add))
                                pool.release(jp, s_ev, s_ev.n)

                t_kev = s_ev.n
                t_kes = s_es.n
                if tt + 1 < NTT:
                    t_xln_next = do_ln2(tt + 1)
                for t1 in range(4):
                    first_chunk = (tt == 0 and t1 == 0)
                    last_chunk = (tt == NTT - 1 and t1 == 3)
                    for h in range(NH):
                        last_of_tile = (t1 == 3 and h == NH - 1)
                        gf = first_chunk and (h % 4 == 0)
                        gl = last_chunk and (h % 4 == 3)
                        P.emit("tensor", ((s_es, t_kes), (s_ev, t_kev)),
                               lambda eng, t1=t1, h=h, gf=gf, gl=gl: eng.matmul(
                                   pkv[:, h * 128:(h + 1) * 128],
                                   ktm[:, t1 * D + h * 128:t1 * D + h * 128 + 128],
                                   vtm[:, t1 * D + h * 128:t1 * D + h * 128 + 128],
                                   start=gf, stop=gl),
                               inc=(s_mm if (last_chunk and h == NH - 1) else None))
                        ksf = first_chunk and (h == 0)
                        ksl = last_chunk and (h == NH - 1)
                        P.emit("tensor", (),
                               lambda eng, t1=t1, h=h, ksf=ksf, ksl=ksl: eng.matmul(
                                   pks[:, h:h + 1],
                                   ktm[:, t1 * D + h * 128:t1 * D + h * 128 + 128],
                                   ones_col_bf[:, :],
                                   start=ksf, stop=ksl),
                               inc=(s_mm if last_of_tile else None))
                prev_kv_tile = s_mm.n
                if t_xln_next is not None:
                    t_xln = t_xln_next
            t_kvdone = s_mm.n

            kvp_v = pkv[:, :].rearrange("p (h e) -> p h e", h=NH)
            kvs_v = kv_sb[:, :].rearrange("p (h e) -> p h e", e=129)
            P.emit("vector", ((s_mm, t_kvdone),), lambda eng: eng.tensor_copy(
                out=kvs_v[:, :, 0:128], in_=kvp_v))
            P.emit("vector", (), lambda eng: eng.tensor_copy(
                out=kvs_v[:, :, 128:129], in_=pks[:, :].rearrange("p h -> p h ()")))
            t_kvsb = s_ev.n
            P.emit("gpsimd", ((s_ev, t_kvsb),), lambda eng: eng.dma_start(
                out=kv_in[:, :], in_=kv_sb[:, :]), inc=s_kv, inc_by=16)
            t_kvst = s_kv.n
            P.emit("gpsimd", ((s_kv, t_kvst),), lambda eng: eng.collective_compute(
                "AllReduce", OP.add, replica_groups=[list(range(NCORE))],
                ins=[kv_in], outs=[kv_out]), inc=s_ar)
            t_ar = s_ar.n
            P.emit("sync", ((s_ar, t_ar),), lambda eng: eng.dma_start(
                out=kv_sb[:, :], in_=kv_out[:, :]), inc=s_kvld, inc_by=16)
            t_kvld = s_kvld.n
            P.emit("vector", ((s_kvld, t_kvld),), lambda eng: eng.tensor_copy(
                out=kv_bf[:, :], in_=kv_sb[:, :]))
        es.close()

        t_kvbf = s_ev.n
        barrier()

        es = _ctx.ExitStack()
        SA = lambda n, sh, dt: es.enter_context(nc.sbuf_tensor(n, list(sh), dt))
        att = SA("a2_att", (128, NKC * TT), BF16)
        xrbufs = [SA("a2_xr%d" % i, (128, TT), F32) for i in range(3)]
        q_all = SA("a2_qall", (128, NH * TT), BF16)
        z_all = SA("a2_zall", (16, TT), F32)
        ksbd = SA("a2_ksbd", (128, NH * 16), BF16)
        sel = SA("a2_sel", (16, NH * 128), F32)
        zrs = [SA("a2_zr%d" % i, (128, TT), F32) for i in range(2)]
        obufs = [SA("a2_ob%d" % i, (128, TT), F32) for i in range(2)]
        wos = [SA("a2_wo%d" % i, (128, NKC * 256), BF16) for i in range(3)]
        pool = PsumPool([es.enter_context(nc.psum_tensor("a2_p%d" % i, [128, TT], F32))
                         for i in range(6)])
        if True:
            xring = Ring(sd[9:12])
            qg_sems = sd[0:4]
            wo_ring = Ring(sd[4:7])
            ob_ring = Ring(ss[4:6])
            dst2 = x2_scr if N_PHASE >= 3 else y_out
            prev_att = 0
            prev_zr = {}
            prev_den = 0
            prev_q = [()] * 4

            P.emit("vector", ((s_ev, t_kvbf),), lambda eng: eng.memset(ksbd[:, :], 0.0))
            for m in range(NH):
                P.emit("vector", (), lambda eng, m=m: eng.tensor_copy(
                    out=ksbd[:, m * 16 + m:m * 16 + m + 1],
                    in_=kv_bf[:, m * 129 + 128:m * 129 + 129]))
            P.emit("sync", (), lambda eng: eng.dma_start(
                out=sel[:, :], in_=sel_in[:, :]), inc=sd[7], inc_by=16)
            t_selld = sd[7].n
            P.emit("vector", ((sd[7], t_selld),), lambda eng: eng.nop())
            t_sel = s_ev.n

            for tt in range(NTT):
                t_qg = [0] * 4
                for h in range(NH):
                    gsl = h // 4
                    wq_ = prev_q[gsl] if h % 4 == 0 else ()
                    t_qg[gsl] = P.emit("sync", tuple(wq_) + (store_tgt[("q", tt, h)],),
                                       lambda eng, h=h, tt=tt: eng.dma_start(
                                           out=q_all[:, h * TT:(h + 1) * TT], in_=q_scr[h, tt]),
                                       inc=qg_sems[gsl], inc_by=16)
                jpd, psden, wpd = pool.acquire()
                for m in range(NH):
                    w = [(qg_sems[m // 4], t_qg[m // 4]), (s_ev, t_sel)]
                    if m == 0:
                        w += list(wpd)
                    P.emit("tensor", tuple(w), lambda eng, m=m, ps=psden: eng.matmul(
                        ps[0:16, :], ksbd[:, m * 16:(m + 1) * 16],
                        q_all[:, m * TT:(m + 1) * TT],
                        start=(m == 0), stop=(m == NH - 1)), inc=s_mm)
                t_den = s_mm.n
                P.emit("vector", ((s_mm, t_den), (s_ev, prev_den)),
                       lambda eng, ps=psden: eng.tensor_scalar(
                           out=z_all[:, :], in0=ps[0:16, :], scalar1=EPS_ATT,
                           scalar2=None, op0=OP.add))
                P.emit("vector", (), lambda eng: eng.reciprocal(
                    out=z_all[:, :], in_=z_all[:, :]))
                t_z = s_ev.n
                prev_den = t_z
                pool.release(jpd, s_ev, t_z)

                for h in range(NH):
                    jp2, psz, wp2 = pool.acquire()
                    P.emit("tensor", ((s_ev, t_z),) + tuple(wp2),
                           lambda eng, h=h, ps=psz: eng.matmul(
                               ps[:, :], sel[:, h * 128:(h + 1) * 128], z_all[:, :],
                               start=True, stop=True), inc=s_mm)
                    t_zrep = s_mm.n
                    P.emit("vector", ((s_mm, t_zrep), prev_zr.get(h % 2)),
                           lambda eng, h=h, ps=psz: eng.tensor_copy(
                               out=zrs[h % 2][:, :], in_=ps[:, :]))
                    t_zr = s_ev.n
                    pool.release(jp2, s_ev, t_zr)
                    jp3, psn, wp3 = pool.acquire()
                    P.emit("tensor", tuple(wp3), lambda eng, h=h, ps=psn: eng.matmul(
                        ps[:, :], kv_bf[:, h * 129:h * 129 + 128],
                        q_all[:, h * TT:(h + 1) * TT],
                        start=True, stop=True), inc=s_mm)
                    t_num = s_mm.n
                    prev_zr[h % 2] = (s_mm, t_num)
                    P.emit("vector", ((s_mm, t_num), (s_ev, t_zr), (s_mm, prev_att)),
                           lambda eng, h=h, ps=psn: eng.tensor_tensor(
                               out=att[:, h * TT:(h + 1) * TT], in0=ps[:, :],
                               in1=zrs[h % 2][:, :], op=OP.mult))
                    pool.release(jp3, s_ev, s_ev.n)
                t_att = s_ev.n
                for gsl in range(4):
                    prev_q[gsl] = ((s_mm, s_mm.n),)

                for dop in range(8):
                    jw, wsem, ww = wo_ring.acquire()
                    t_wo_ = P.emit("sync", tuple(ww) + ((s_cast, t_cast),),
                                   lambda eng, jw=jw, dop=dop: eng.dma_start(
                                       out=wos[jw][:, :].rearrange("p (m c) -> p m c", c=256),
                                       in_=wo_b[dop].rearrange("m p c -> p m c")),
                                   inc=wsem, inc_by=16)
                    for s in range(2):
                        do = dop * 2 + s
                        jp, pso, wp = pool.acquire()
                        for e in range(NKC):
                            w = ((wsem, t_wo_), (s_ev, t_att))
                            if e == 0:
                                w = w + tuple(wp)
                            P.emit("tensor", w, lambda eng, e=e, jw=jw, s=s, ps=pso: eng.matmul(
                                ps[:, :], wos[jw][:, e * 256 + s * 128:e * 256 + s * 128 + 128],
                                att[:, e * TT:(e + 1) * TT],
                                start=(e == 0), stop=(e == NKC - 1)), inc=s_mm)
                        t_omm = s_mm.n
                        if s == 1:
                            wo_ring.release(jw, s_mm, t_omm)
                        jx, xsem, wx = xring.acquire()
                        tl = P.emit("sync", tuple(wx) + (store_tgt[("x1", tt, do)],),
                                    lambda eng, do=do, jx=jx, tt=tt: eng.dma_start(
                                        out=xrbufs[jx][:, :],
                                        in_=x1_scr[do * 128:(do + 1) * 128, tt * TT:(tt + 1) * TT]),
                                    inc=xsem, inc_by=16)
                        jo, osem, wo2 = ob_ring.acquire()
                        P.emit("vector", ((s_mm, t_omm), (xsem, tl), (s_const, t_const)) + tuple(wo2),
                               lambda eng, do=do, jx=jx, jo=jo, ps=pso: eng.scalar_tensor_tensor(
                                   out=obufs[jo][:, :], in0=ps[:, :],
                                   scalar=boc[:, do:do + 1], in1=xrbufs[jx][:, :],
                                   op0=OP.add, op1=OP.add))
                        t_x2 = s_ev.n
                        xring.release(jx, s_ev, t_x2)
                        pool.release(jp, s_ev, t_x2)
                        tstv = P.emit("gpsimd", ((s_ev, t_x2),),
                                      lambda eng, do=do, tt=tt, jo=jo: eng.dma_start(
                                          out=dst2[do * 128:(do + 1) * 128, tt * TT:(tt + 1) * TT],
                                          in_=obufs[jo][:, :]),
                                      inc=osem, inc_by=16)
                        ob_ring.release(jo, osem, tstv)
                        store_tgt[("x2", tt, do)] = (osem, tstv)
                prev_att = s_mm.n
        es.close()
        es0.close()

    if N_PHASE >= 1:
        emit_ffn(x_in, "x", x1_scr if N_PHASE >= 2 else y_out, "x1", "ffn1")
        emit_cast_jobs(100)
        assert not cast_jobs and s_cast.n == t_cast
        barrier()
    if N_PHASE >= 2:
        emit_attn()
        barrier()
    if N_PHASE >= 3:
        emit_ffn(x2_scr, "x2", y_out, "y", "ffn2")

    fin = tuple((s, s.n) for s in ([s_kv] + ss))
    P.emit("gpsimd", fin, lambda eng: eng.nop())
    P.emit("sync", fin, lambda eng: eng.nop())

    with nc.Block() as block:
        nc.all_engine_barrier()
        for sm in all_sems:
            nc.gpsimd.sem_clear(sm.h)
        nc.all_engine_barrier()

        @block.sync
        def _(eng):
            for fn in P.q["sync"]:
                fn(eng)

        @block.tensor
        def _(eng):
            for fn in P.q["tensor"]:
                fn(eng)

        @block.vector
        def _(eng):
            for fn in P.q["vector"]:
                fn(eng)

        @block.scalar
        def _(eng):
            for fn in P.q["scalar"]:
                fn(eng)

        @block.gpsimd
        def _(eng):
            for fn in P.q["gpsimd"]:
                fn(eng)

    stack.close()
    return nc


_NC_CACHE = None


def kernel(**inputs):
    global _NC_CACHE
    import ml_dtypes

    x = np.asarray(inputs["L0_attn_x"])
    Wq = np.asarray(inputs["L0_attn_Wq"]); bq = np.asarray(inputs["L0_attn_bq"])
    Wk = np.asarray(inputs["L0_attn_Wk"]); bk = np.asarray(inputs["L0_attn_bk"])
    Wv = np.asarray(inputs["L0_attn_Wv"]); bv = np.asarray(inputs["L0_attn_bv"])
    Wo = np.asarray(inputs["L0_attn_Wo"]); bo = np.asarray(inputs["L0_attn_bo"])
    up = np.asarray(inputs["L0_ffn_up_kernel"]); bup = np.asarray(inputs["L0_ffn_up_bias"])
    dn = np.asarray(inputs["L0_ffn_down_kernel"]); bdn = np.asarray(inputs["L0_ffn_down_bias"])

    C = np.ascontiguousarray
    xT = C(x.reshape(SEQ, D).T)
    wqt = C(Wq.T); wkt = C(Wk.T); wvt = C(Wv.T); wot = C(Wo.T)
    upT = C(up.T)
    dnT = C(dn.T)
    pos = np.arange(SEQ, dtype=np.float32)
    eye = np.eye(128, dtype=ml_dtypes.bfloat16)
    ladder = np.broadcast_to(
        np.exp2(np.arange(EMIN, EMAX + 1, dtype=np.float32)), (128, EMAX - EMIN + 1)).copy()
    selmat = np.zeros((16, NH * 128), dtype=np.float32)
    for _h in range(NH):
        selmat[_h, _h * 128:(_h + 1) * 128] = 1.0

    def col(b, nch):
        return C(b.reshape(nch, 128).T.astype(np.float32))

    if _NC_CACHE is None:
        _NC_CACHE = build_program()
    nc = _NC_CACHE

    in_maps = []
    for c in range(NCORE):
        sl = slice(c * TCORE, (c + 1) * TCORE)
        in_maps.append({
            "x": C(xT[:, sl]),
            "wqt": wqt, "wkt": wkt, "wvt": wvt, "wot": wot,
            "upt": C(upT[c * (FF // NCORE):(c + 1) * (FF // NCORE)]),
            "dnt": C(dnT[c * (D // NCORE):(c + 1) * (D // NCORE)]),
            "bq_col": col(bq, NKC), "bo_col": col(bo, NKC),
            "bk_row": C(bk.reshape(1, D)), "bv_row": C(bv.reshape(1, D)),
            "bup_col": col(bup, NFC), "bdn_col": col(bdn, NKC),
            "pos_row": C(pos[sl].reshape(1, TCORE)),
            "pos_col": C(pos[sl].reshape(16, 128).T),
            "ladder": ladder,
            "sel": selmat,
            "eye": eye,
        })

    res = run_bass_kernel_spmd(nc, in_maps, core_ids=list(range(NCORE)))
    out = np.empty((SEQ, D), dtype=np.float32)
    for c in range(NCORE):
        out[c * TCORE:(c + 1) * TCORE] = res.results[c]["y"].T
    return out.reshape(1, SEQ, D)
